# version 01
# baseline (speedup 1.0000x reference)
import contextlib
import dataclasses
import math
import numpy as np

import concourse.bass as bass
import concourse.mybir as mybir
from concourse.bass_utils import run_bass_kernel_spmd
from concourse.expressions import RuntimeValue

F32 = mybir.dt.float32
BF16 = mybir.dt.bfloat16
I32 = mybir.dt.int32
U8 = mybir.dt.uint8
AF = mybir.ActivationFunctionType
ALU = mybir.AluOpType
AX = mybir.AxisListType
ds = bass.ds

MAGIC = float(3 * 2 ** 22)
EXPB = -30.0
ISQH = 1.0 / math.sqrt(128.0)
D = 2048
NTC = 8
NDC = 16
HFF = 5632
NQH = 11
NMEL = float(HFF * D)


def sap(ap, off, dims):
    return dataclasses.replace(ap, offset=off, ap=dims)


class KB:
    def __init__(self, upto, dbg_tag=None):
        self.upto = upto
        self.dbg_tag = dbg_tag
        self.nc = bass.Bass("TRN2", target_bir_lowering=False, debug=False)
        self.stx = contextlib.ExitStack()
        self.par = {}
        self.regs = {}

    def declare(self):
        nc = self.nc
        self.xin = nc.dram_tensor("xin", [1024, D], F32, kind="ExternalInput").ap()
        self.wd = {}
        for nm, shp in [("l0wq", [D, D]), ("l0wk", [D, 512]), ("l0wv", [D, 512]),
                        ("l0wo", [D, D]), ("l0fk", [D, D]), ("l1wq", [D, D]),
                        ("l1wk", [D, D]), ("l1wv", [D, D]), ("l1wo", [D, D]),
                        ("w1", [HFF, D]), ("w2", [HFF, D]), ("w3", [D, HFF])]:
            self.wd[nm] = nc.dram_tensor(nm, shp, F32, kind="ExternalInput").ap()
        self.l0fb = nc.dram_tensor("l0fb", [1, D], F32, kind="ExternalInput").ap()
        self.rmsw = nc.dram_tensor("rmsw", [128, NDC], F32, kind="ExternalInput").ap()
        self.brow = {nm: nc.dram_tensor(nm, [1, D], F32, kind="ExternalInput").ap()
                     for nm in ("bq", "bk", "bv", "bo")}
        self.t_l0 = nc.dram_tensor("t_l0", [128, 2 * NTC * 128], F32,
                                   kind="ExternalInput").ap()
        self.t_l1 = nc.dram_tensor("t_l1", [128, 2 * NTC * 128], F32,
                                   kind="ExternalInput").ap()
        self.mkin = nc.dram_tensor("mkin", [128, 16 * 512], BF16,
                                   kind="ExternalInput").ap()
        self.identf = nc.dram_tensor("identf", [128, 128], F32,
                                     kind="ExternalInput").ap()
        self.tokin = nc.dram_tensor("tokin", [8, 16], I32, kind="ExternalInput").ap()
        self.out = nc.dram_tensor("out", [1024, D], F32, kind="ExternalOutput").ap()
        self.dbg = nc.dram_tensor("dbg", [128, 8192], F32, kind="ExternalOutput").ap()

        sh = lambda n, s, d: nc.dram_tensor(n, s, d, kind="Internal",
                                            addr_space="Shared").ap()
        self.kv0 = sh("kv0", [2048, 1024], BF16)
        self.kv1 = sh("kv1", [2048, 4096], BF16)
        self.wq_sh = {"w1": sh("w1q_sh", [HFF, D], BF16),
                      "w2": sh("w2q_sh", [HFF, D], BF16),
                      "w3": sh("w3q_sh", [D, HFF], BF16)}
        self.sfk_sh = sh("sfk_sh", [2, D], F32)
        self.gam_sh = sh("gam_sh", [2, 16], F32)
        self.flf = sh("fl_sh", [16, 16], I32)
        self.scr = nc.dram_tensor("scr_row", [1, D], F32, kind="Internal").ap()

    def alloc(self):
        nc, stx = self.nc, self.stx
        sb = lambda n, s, d: stx.enter_context(nc.sbuf_tensor(n, s, d))
        self.x_res = sb("x_res", [128, NTC * D], F32)
        self.fm = sb("fm", [128, NDC * 1024], BF16)
        self.tm = sb("tm", [128, NTC * D], BF16)
        self.arena = sb("arena", [128, 16384], BF16)
        self.tabs = sb("tabs", [128, 2048], F32)
        self.ro32 = sb("ro32", [128, 2048], F32)
        self.r32b = sb("r32b", [128, 2048], F32)
        self.wbuf = sb("wbuf", [128, NDC * 512], BF16)
        self.eb = sb("eb", [128, 4 * 512], BF16)
        self.rlrow = sb("rlrow", [1, 1024], BF16)
        self.ident = sb("ident", [128, 128], BF16)
        self.onescol = sb("onescol", [128, 1], BF16)
        self.onesrow = sb("onesrow", [1, 128], BF16)
        self.stats = sb("stats", [128, 64], F32)
        self.stt_b = sb("stt_b", [128, 24], F32)
        self.srow = sb("srow", [1, 64], F32)
        self.rmsc = sb("rmsc", [128, NDC], F32)
        self.sca = sb("sca", [128, 64], F32)
        self.gcol = sb("gcol", [128, 8], F32)
        self.cbias = sb("cbias", [128, 2], F32)
        self.ftile = sb("ftile", [1, 16], I32)
        self.tokt = sb("tokt", [8, 16], I32)
        self.ro = self.eb
        self.bqb = self.eb
        self.rowf = lambda a, b: self.ro32[0:1, a:b]
        self.sfb = self.tabs
        self.sfb2 = self.r32b
        self.pingB = self.arena[:, 0:8192]
        self.k0b = lambda g: self.arena[:, g * 2048:(g + 1) * 2048]
        self.v0b = lambda g: self.arena[:, 8192 + g * 2048:8192 + (g + 1) * 2048]
        self.k1r = lambda s: self.arena[:, s * 2048:(s + 1) * 2048]
        self.v1r = lambda s: self.arena[:, 4096 + s * 2048:4096 + (s + 1) * 2048]
        self.maskv = lambda i: self.arena[:, 8192 + i * 512:8192 + (i + 1) * 512]
        self.wt3 = self.arena[:, 0:5632]
        self.w2buf = self.arena[:, 5632:13824]
        self.wf32 = self.ro32[:, 0:1024]
        self.idf32 = self.ro32[:, 1152:1280]
        self.qscr = self.ro32[:, 1024:1536]

        self.ps = [stx.enter_context(nc.psum_tensor(f"ps{i}", [128, 512], F32))
                   for i in range(8)]
        self.sem = {e: stx.enter_context(nc.semaphore("sem_" + e))
                    for e in ("t", "v", "a", "g", "y")}
        self.psem = stx.enter_context(nc.semaphore("sem_poll"))
        self.block = stx.enter_context(nc.Block())
        self.cnt = {e: 0 for e in self.sem}

    def inc(self, instr, e, k=1):
        instr.then_inc(self.sem[e], k)
        self.cnt[e] += k
        return self.cnt[e]

    def w(self, eng, e, val):
        assert val <= self.cnt[e], (e, val, self.cnt[e])
        if val > 0:
            eng.wait_ge(self.sem[e], val)

    def snap(self, e):
        return self.cnt[e]

    def vchain(self, v, emits):
        for f in emits:
            i = f()
            self.inc(i, "v", 1)
            self.w(v, "v", self.snap("v"))

    def wait_now(self, eng, *engines):
        for e in engines:
            self.w(eng, e, self.snap(e))

    def xr(self, tc, c0=0, n=D):
        return self.x_res[:, tc * D + c0: tc * D + c0 + n]

    def fmv(self, dc, c0, n):
        return self.fm[:, dc * 1024 + c0: dc * 1024 + c0 + n]

    def tmv(self, tc, c0, n):
        return self.tm[:, tc * D + c0: tc * D + c0 + n]

    def psb(self, i, c0, n):
        return self.ps[i][:].bitcast(BF16)[:, c0:c0 + n]

    def ebs(self, n):
        return self.eb[:, (n % 4) * 512:((n % 4) + 1) * 512]

    def mcol(self, tc):
        return self.stats[:, tc * 4:tc * 4 + 1]

    def rcol(self, tc):
        return self.stats[:, tc * 4 + 3:tc * 4 + 4]

    def rccol(self, tc):
        return self.stats[:, tc * 4 + 2:tc * 4 + 3]

    def get_par(self, eng, key):
        if key not in self.par:
            pid = eng.partition_id()
            self.par[key] = pid % 2
            self.par[key + "_not"] = (pid + 1) % 2
        return self.par[key]

    def get_notpar(self, eng, key):
        self.get_par(eng, key)
        return self.par[key + "_not"]

    BASES = {
        "kv0A": (True, 1048576, 0), "kv0B": (False, 1048576, 2097152),
        "kv1A": (True, 4194304, 0), "kv1B": (False, 4194304, 8388608),
        "p64": (True, 64, 0), "n64": (False, 64, 0),
        "p8192": (True, 8192, 0), "n8192": (False, 8192, 0),
        "l0fk": (True, 8388608, 0),
        "pbig4": (True, 23068672, 0), "pbig2": (True, 11534336, 0),
    }

    def dyn_init(self, y):
        if "rbases" in self.regs:
            return
        pid = y.partition_id()
        rpar = y.alloc_register("rpar")
        y.reg_mod(rpar, pid.val, 2)
        rnpar = y.alloc_register("rnpar")
        y.reg_add(rnpar, rpar, 1)
        y.reg_mod(rnpar, rnpar, 2)
        R = {}
        for name, (use_par, mul, add) in self.BASES.items():
            r = y.alloc_register("rb_" + name)
            y.reg_mul(r, rpar if use_par else rnpar, mul)
            if add:
                y.reg_add(r, r, add)
            R[name] = r
        R["roff"] = y.alloc_register("roff")
        self.regs["rbases"] = R

    DBOUND = {
        "kv0": (0, 3933184), "kv1": (0, 15732736), "flf": (0, 960),
        "sfk": (0, 8192), "gam": (0, 64), "l0fk": (0, 15728640),
        "w12": (0, 45088768), "w3a": (0, 43268096), "w3b": (0, 43270144),
        "w12s": (0, 22544384), "w3as": (0, 21634048), "w3bs": (0, 21635072),
    }

    def dyn_ap(self, y, static_ap, base, delta_elems, bclass):
        self.dyn_init(y)
        es = {F32: 4, BF16: 2, I32: 4}[static_ap.dtype]
        u8ap = static_ap.bitcast(U8)
        R = self.regs["rbases"]
        y.reg_add(R["roff"], R[base], delta_elems * es)
        lo, hi = self.DBOUND[bclass]
        rv = RuntimeValue(val=R["roff"], min_val=lo, max_val=hi)
        return dataclasses.replace(u8ap, offset=rv)

    def p0_setup(self):
        K = self

        def _(y):
            for tc in range(NTC):
                K.inc(y.dma_start(out=K.xr(tc),
                                  in_=K.xin[tc * 128:(tc + 1) * 128, :]), "y", 16)
            K.inc(y.dma_start(out=K.tabs[:], in_=K.t_l0), "y", 16)
            K.inc(y.dma_start(out=K.rmsc[:], in_=K.rmsw), "y", 16)
            K.inc(y.dma_start(out=K.tokt[:], in_=K.tokin), "y", 16)
        K.block.sync(_)
        K.ev_p0y = K.snap("y")

        def _(g):
            K.inc(g.dma_start(out=K.ident[:], in_=K.identf), "g", 16)
        K.block.gpsimd(_)
        K.ev_ident = K.snap("g")

        def _(v):
            v.memset(K.onescol[:], 1.0)
            v.memset(K.ftile[:], 0)
            v.memset(K.cbias[:, 0:1], 1e-5)
            v.memset(K.cbias[:, 1:2], EXPB)
            K.inc(v.memset(K.onesrow[:], 1.0), "v", 1)
        K.block.vector(_)
        K.ev_ones = K.snap("v")

    def ln_stats(self, extra_rms=False):
        K = self

        def _(v):
            K.w(v, "y", K.ev_p0y)
            for tc in range(NTC):
                i = None
                for c in range(4):
                    i = v.bn_stats(K.stt_b[:, c * 6:(c + 1) * 6],
                                   K.xr(tc, c * 512, 512))
                K.inc(i, "v", 1)
                K.w(v, "v", K.snap("v"))
                K.inc(v.bn_aggr(K.stats[:, tc * 4:tc * 4 + 2], K.stt_b[:, 0:24]),
                      "v", 1)
                K.w(v, "v", K.snap("v"))
        K.block.vector(_)
        ev_v = K.snap("v")

        def _(a):
            K.w(a, "v", ev_v)
            varv = sap(K.stats[:], 1, [[64, 128], [4, NTC]])
            sq = sap(K.stats[:], 2, [[64, 128], [4, NTC]])
            K.inc(a.activation(sq, varv, AF.Sqrt, bias=self.cbias[:, 0:1]), "a", 1)
        K.block.scalar(_)
        ev_a = K.snap("a")

        def _(v):
            K.w(v, "a", ev_a)
            sq = sap(K.stats[:], 2, [[64, 128], [4, NTC]])
            rv = sap(K.stats[:], 3, [[64, 128], [4, NTC]])
            K.inc(v.reciprocal(rv, sq), "v", 1)
            K.w(v, "v", K.snap("v"))
            if extra_rms:
                varv = sap(K.stats[:], 1, [[64, 128], [4, NTC]])
                K.vchain(v, [
                    lambda: v.tensor_tensor(K.sca[:, 48:56], rv, rv, ALU.mult),
                    lambda: v.tensor_tensor(K.sca[:, 48:56], K.sca[:, 48:56],
                                            varv, ALU.mult)])
        K.block.vector(_)
        if not extra_rms:
            return K.snap("v")
        ev_v2 = K.snap("v")

        def _(a):
            K.w(a, "v", ev_v2)
            K.inc(a.activation(K.sca[:, 56:64], K.sca[:, 48:56], AF.Sqrt,
                               bias=self.cbias[:, 0:1]), "a", 1)
        K.block.scalar(_)
        ev_a2 = K.snap("a")

        def _(v):
            K.w(v, "a", ev_a2)
            rv = sap(K.stats[:], 3, [[64, 128], [4, NTC]])
            rc = sap(K.stats[:], 2, [[64, 128], [4, NTC]])
            mnv = sap(K.stats[:], 0, [[64, 128], [4, NTC]])
            K.vchain(v, [
                lambda: v.reciprocal(K.sca[:, 48:56], K.sca[:, 56:64]),
                lambda: v.tensor_tensor(rc, rv, K.sca[:, 48:56], ALU.mult),
                lambda: v.tensor_scalar(K.sca[:, 32:40], rv, 1.0, None, ALU.add),
                lambda: v.tensor_tensor(K.sca[:, 40:48], mnv, rv, ALU.mult),
                lambda: v.tensor_scalar(K.sca[:, 40:48], K.sca[:, 40:48], -1.0,
                                        None, ALU.mult)])
        K.block.vector(_)
        return K.snap("v")

    def apply_norm(self, scol_fn):
        K = self
        ynow = K.snap("y")

        def _(v):
            K.w(v, "y", ynow)
            for tc in range(NTC):
                K.inc(v.tensor_scalar(K.tmv(tc, 0, D), K.xr(tc), K.mcol(tc),
                                      scol_fn(tc), ALU.subtract, ALU.mult), "v", 1)
        K.block.vector(_)
        return K.snap("v")

    def transposes(self, src_fn, ev_src, scol_fn=None):
        K = self
        seq = [(dc, g4) for dc in range(NDC) for g4 in range(2)]
        a_done = []
        a_prev = K.snap("a")
        for n, (dc, g4) in enumerate(seq):
            def mk_t(n=n, dc=dc, g4=g4):
                def _(t):
                    if n == 0:
                        if isinstance(ev_src, tuple):
                            K.w(t, *ev_src)
                        else:
                            K.w(t, "v", ev_src)
                        K.w(t, "v", K.ev_ones)
                        K.w(t, "g", K.ev_ident)
                        K.w(t, "a", a_prev)
                    bank = 4 + n % 2
                    if n >= 2:
                        K.w(t, "a", a_done[n - 2])
                    i = None
                    for j in range(4):
                        tc = g4 * 4 + j
                        i = t.transpose(K.psb(bank, j * 128, 128),
                                        src_fn(tc)[:, dc * 128:(dc + 1) * 128],
                                        K.ident[:])
                    K.inc(i, "t", 1)
                return _
            K.block.tensor(mk_t())
            tsnap = K.snap("t")

            def mk_a(n=n, dc=dc, g4=g4, tsnap=tsnap):
                def _(a):
                    bank = 4 + n % 2
                    K.w(a, "t", tsnap)
                    dstv = K.fmv(dc, g4 * 512, 512)
                    if scol_fn is None:
                        i = a.copy(dstv, K.psb(bank, 0, 512))
                    else:
                        i = a.activation(dstv, K.psb(bank, 0, 512), AF.Copy,
                                         scale=scol_fn(dc))
                    a_done.append(K.inc(i, "a", 1))
                return _
            K.block.scalar(mk_a())
        return K.snap("a")

    def proj_tm(self, wdram, ncols, dst_fn, ev_fm, bias_dram=None,
                extra_first_wait=None):
        K = self
        nspans = ncols // 512
        ev_bias = 0
        if bias_dram is not None:
            vnow = K.snap("v")

            def _(g):
                K.w(g, "v", vnow)
                K.inc(g.dma_start(out=K.bqb[:],
                                  in_=sap(bias_dram, 0, [[0, 128], [1, D]])),
                      "g", 16)
            K.block.gpsimd(_)
            ev_bias = K.snap("g")
        bank_hist = {}
        ev_mm_span = {}
        tprev = K.snap("t")
        for fs in range(nspans):
            buf = K.wbuf if fs % 2 == 0 else K.pingB

            def mk_g(fs=fs, buf=buf):
                def _(g):
                    K.w(g, "t", ev_mm_span.get(fs - 2, tprev))
                    for dc in range(NDC):
                        K.inc(g.dma_start(
                            out=buf[:, dc * 512:(dc + 1) * 512],
                            in_=wdram[dc * 128:(dc + 1) * 128,
                                      fs * 512:(fs + 1) * 512]), "g", 16)
                return _
            K.block.gpsimd(mk_g())
            ev_w = K.snap("g")
            for tc in range(NTC):
                bank = tc % 4

                def mk_t(fs=fs, tc=tc, bank=bank, buf=buf, ev_w=ev_w):
                    def _(t):
                        if tc == 0:
                            K.w(t, "g", ev_w)
                            if fs == 0:
                                K.w(t, "a", ev_fm)
                                if extra_first_wait is not None:
                                    K.w(t, *extra_first_wait)
                        if bank in bank_hist:
                            K.w(t, "v", bank_hist[bank])
                        i = None
                        for dc in range(NDC):
                            i = t.matmul(K.ps[bank][:], K.fmv(dc, tc * 128, 128),
                                         buf[:, dc * 512:(dc + 1) * 512],
                                         start=(dc == 0), stop=(dc == NDC - 1))
                        K.inc(i, "t", 1)
                    return _
                K.block.tensor(mk_t())
                ev_mm_span[fs] = K.snap("t")
                tsnap = K.snap("t")

                def mk_v(fs=fs, tc=tc, bank=bank, tsnap=tsnap):
                    def _(v):
                        K.w(v, "t", tsnap)
                        if bias_dram is not None:
                            K.w(v, "g", ev_bias)
                            i = v.tensor_tensor(dst_fn(tc, fs), K.ps[bank][:],
                                                K.bqb[:, fs * 512:(fs + 1) * 512],
                                                ALU.add)
                        else:
                            i = v.tensor_copy(dst_fn(tc, fs), K.ps[bank][:])
                        bank_hist[bank] = K.inc(i, "v", 1)
                    return _
                K.block.vector(mk_v())
        return K.snap("v")

    def rope(self, kind, nh, src_ap, ev_src, tc):
        K = self

        def _(v):
            if isinstance(ev_src, tuple):
                K.w(v, *ev_src)
            else:
                K.w(v, "v", ev_src)
            pr = src_ap.ap[0][0]
            t0 = sap(K.tabs[:], (0 * NTC + tc) * 128, [[2048, 128], [0, nh], [1, 128]])
            t1 = sap(K.tabs[:], (1 * NTC + tc) * 128, [[2048, 128], [0, nh], [1, 128]])
            o1 = sap(K.ro32[:], 0, [[2048, 128], [128, nh], [1, 128]])
            o2 = sap(K.r32b[:], 0, [[2048, 128], [128, nh], [1, 128]])
            if kind == 0:
                plain = sap(src_ap, src_ap.offset, [[pr, 128], [128, nh], [1, 128]])
                rot = sap(src_ap, src_ap.offset + 64,
                          [[pr, 128], [128, nh], [-64, 2], [1, 64]])
                v.tensor_tensor(o1, plain, t0, ALU.mult)
                v.tensor_tensor(o2, rot, t1, ALU.mult)
            else:
                evn = sap(src_ap, src_ap.offset,
                          [[pr, 128], [128, nh], [0, 2], [2, 64]])
                odd = sap(src_ap, src_ap.offset + 1,
                          [[pr, 128], [128, nh], [0, 2], [2, 64]])
                v.tensor_tensor(o1, evn, t0, ALU.mult)
                v.tensor_tensor(o2, odd, t1, ALU.mult)
            i = v.tensor_tensor(K.ro[:, 0:nh * 128], K.ro32[:, 0:nh * 128],
                                K.r32b[:, 0:nh * 128], ALU.add)
            K.inc(i, "v", 1)
        K.block.vector(_)
        return K.snap("v")

    def flag_write(self, p, ev_y_data):
        K = self

        def _(y):
            K.w(y, "y", ev_y_data)
            ap = K.dyn_ap(y, K.flf[0:1, 0:16], "p64", p * 32, "flf")
            K.inc(y.dma_start(out=ap, in_=K.tokt[p:p + 1, :].bitcast(U8)), "y", 16)
        K.block.sync(_)
        return K.snap("y")

    def flag_poll(self, p):
        K = self

        def _(y):
            if "fr" not in self.regs:
                self.regs["fr"] = y.alloc_register("fr")
                self.regs["tr"] = y.alloc_register("tr")
            fr, tr = self.regs["fr"], self.regs["tr"]
            K.w(y, "v", K.ev_ones)
            y.load(tr, K.tokt[p:p + 1, 0:1])
            ap = K.dyn_ap(y, K.flf[0:1, 0:16], "n64", p * 32, "flf")

            def cond():
                y.load(fr, K.ftile[0:1, 0:1])
                return RuntimeValue(val=fr) != RuntimeValue(val=tr)

            with y.While(cond):
                y.sem_clear(K.psem)
                y.dma_start(out=K.ftile[:].bitcast(U8), in_=ap).then_inc(K.psem, 16)
                y.wait_ge(K.psem, 16)
            K.inc(y.nop(), "y", 1)
        K.block.sync(_)
        return K.snap("y")

    def attention(self, ev_ready_y, ev_q_a, kfm_fn, vtm_fn, spans,
                  head_load=None):
        K = self
        first = True
        onorm_hist = {}
        for h in range(16):
            if head_load is not None:
                head_load(h)
            tail_a = {}
            for (spi, kcs, mask_fn) in spans:
                nk = len(kcs)
                a0 = K.snap("a")
                v0 = K.snap("v")
                e_ready = {}
                exp_a = {}
                for n, kc in enumerate(kcs):
                    def mk_t(h=h, spi=spi, kc=kc, n=n, first=first, a0=a0,
                             v0=v0, kcs=kcs, nk=nk):
                        def _(t):
                            if n == 0:
                                if first:
                                    K.w(t, "y", ev_ready_y)
                                    K.w(t, "a", ev_q_a)
                                K.w(t, "a", a0)
                                K.w(t, "v", v0)
                                if head_load is not None:
                                    K.w(t, "y", K.ev_headload[h])
                            if n >= 2:
                                K.w(t, "a", exp_a[n - 2])
                            i = t.matmul(K.ps[n % 2][:], kfm_fn(h, kc),
                                         K.fmv(h, spi * 512, 512),
                                         start=True, stop=True)
                            K.inc(i, "t", 1)
                            if n >= 1:
                                pk = n - 1
                                K.w(t, *e_ready[pk])
                                t.matmul(K.ps[2][0:1, :], K.onescol[:],
                                         K.ebs(pk), start=(pk == 0), stop=False)
                                i = t.matmul(K.ps[3][:], vtm_fn(h, kcs[pk]),
                                             K.ebs(pk), start=(pk == 0),
                                             stop=False)
                                K.inc(i, "t", 1)
                        return _
                    K.block.tensor(mk_t())
                    first = False
                    tsnap = K.snap("t")

                    def mk_a(n=n, tsnap=tsnap):
                        def _(a):
                            K.w(a, "t", tsnap)
                            i = a.activation(K.ebs(n), K.ps[n % 2][:], AF.Exp,
                                             scale=ISQH, bias=K.cbias[:, 1:2])
                            exp_a[n] = K.inc(i, "a", 1)
                        return _
                    K.block.scalar(mk_a())
                    ms = mask_fn(kc) if mask_fn else None
                    if ms is None:
                        e_ready[n] = ("a", exp_a[n])
                    else:
                        asn = exp_a[n]

                        def mk_v(n=n, ms=ms, asn=asn):
                            def _(v):
                                K.w(v, "a", asn)
                                sl = K.ebs(n)
                                i = v.tensor_tensor(sl, sl, K.maskv(ms), ALU.mult)
                                K.inc(i, "v", 1)
                            return _
                        K.block.vector(mk_v())
                        e_ready[n] = ("v", K.snap("v"))

                def mk_tl(nk=nk, kcs=kcs, h=h, e_ready=e_ready):
                    def _(t):
                        pk = nk - 1
                        K.w(t, *e_ready[pk])
                        t.matmul(K.ps[2][0:1, :], K.onescol[:], K.ebs(pk),
                                 start=(pk == 0), stop=True)
                        i = t.matmul(K.ps[3][:], vtm_fn(h, kcs[pk]), K.ebs(pk),
                                     start=(pk == 0), stop=True)
                        K.inc(i, "t", 1)
                    return _
                K.block.tensor(mk_tl())
                tsnap = K.snap("t")

                def mk_ae(h=h, spi=spi, tsnap=tsnap):
                    def _(a):
                        K.w(a, "t", tsnap)
                        a.copy(K.ro32[0:1, spi * 512:(spi + 1) * 512],
                               K.ps[2][0:1, :])
                        i = a.copy(K.tm[:, h * 1024 + spi * 512:
                                        h * 1024 + (spi + 1) * 512],
                                   K.ps[3][:])
                        K.inc(i, "a", 1)
                    return _
                K.block.scalar(mk_ae())
                tail_a[spi] = K.snap("a")

            def mk_rl(h=h, tail_a=tail_a):
                def _(a):
                    K.w(a, "a", max(tail_a.values()))
                    a.activation(K.ro32[0:1, 0:1024], K.ro32[0:1, 0:1024], AF.Ln)
                    i = a.activation(K.rlrow[:], K.ro32[0:1, 0:1024], AF.Exp,
                                     scale=-1.0)
                    K.inc(i, "a", 1)
                return _
            K.block.scalar(mk_rl())
            ev_rl = K.snap("a")
            for spi in range(2):
                n = h * 2 + spi
                bank = 4 + n % 2

                def mk_tb(h=h, spi=spi, n=n, bank=bank, ev_rl=ev_rl):
                    def _(t):
                        K.w(t, "a", ev_rl)
                        if n >= 2:
                            K.w(t, "v", onorm_hist[n - 2])
                        i = t.matmul(K.ps[bank][:], K.onesrow[:],
                                     K.rlrow[0:1, spi * 512:(spi + 1) * 512],
                                     start=True, stop=True)
                        K.inc(i, "t", 1)
                    return _
                K.block.tensor(mk_tb())
                tsnap = K.snap("t")

                def mk_vb(h=h, spi=spi, n=n, bank=bank, tsnap=tsnap):
                    def _(v):
                        K.w(v, "t", tsnap)
                        sl = K.tm[:, h * 1024 + spi * 512:
                                  h * 1024 + (spi + 1) * 512]
                        i = v.tensor_tensor(sl, sl, K.ps[bank][:], ALU.mult)
                        onorm_hist[n] = K.inc(i, "v", 1)
                    return _
                K.block.vector(mk_vb())
        return K.snap("v")

    def proj_out(self, wdram, ev_o_v, bias_dram=None):
        K = self
        ev_bias = 0
        if bias_dram is not None:
            vnow = K.snap("v")
            tnow = K.snap("t")

            def _(g):
                K.w(g, "v", vnow)
                K.w(g, "t", tnow)
                K.inc(g.dma_start(out=K.bqb[:],
                                  in_=sap(bias_dram, 0, [[0, 128], [1, D]])),
                      "g", 16)
            K.block.gpsimd(_)
            ev_bias = K.snap("g")
        bank_hist = {}
        ev_mm_span = {}
        tprev = K.snap("t")
        for fs in range(4):
            buf = K.wbuf if fs % 2 == 0 else K.pingB

            def mk_g(fs=fs, buf=buf):
                def _(g):
                    K.w(g, "t", ev_mm_span.get(fs - 2, tprev))
                    if fs < 2:
                        K.w(g, "v", ev_o_v)
                    for hc in range(NDC):
                        K.inc(g.dma_start(
                            out=buf[:, hc * 512:(hc + 1) * 512],
                            in_=wdram[hc * 128:(hc + 1) * 128,
                                      fs * 512:(fs + 1) * 512]), "g", 16)
                return _
            K.block.gpsimd(mk_g())
            ev_w = K.snap("g")
            for tc in range(NTC):
                bank = tc % 4

                def mk_t(fs=fs, tc=tc, bank=bank, buf=buf, ev_w=ev_w):
                    def _(t):
                        if tc == 0:
                            K.w(t, "g", ev_w)
                            if fs == 0:
                                K.w(t, "v", ev_o_v)
                        if bank in bank_hist:
                            K.w(t, "v", bank_hist[bank])
                        i = None
                        for hc in range(NDC):
                            i = t.matmul(K.ps[bank][:],
                                         K.tm[:, hc * 1024 + tc * 128:
                                              hc * 1024 + (tc + 1) * 128],
                                         buf[:, hc * 512:(hc + 1) * 512],
                                         start=(hc == 0), stop=(hc == NDC - 1))
                        K.inc(i, "t", 1)
                    return _
                K.block.tensor(mk_t())
                ev_mm_span[fs] = K.snap("t")
                tsnap = K.snap("t")

                def mk_v(fs=fs, tc=tc, bank=bank, tsnap=tsnap):
                    def _(v):
                        K.w(v, "t", tsnap)
                        if bias_dram is not None:
                            K.w(v, "g", ev_bias)
                            v.tensor_tensor(K.qscr[:], K.ps[bank][:],
                                            K.bqb[:, fs * 512:(fs + 1) * 512],
                                            ALU.add)
                            i = v.tensor_tensor(K.xr(tc, fs * 512, 512),
                                                K.qscr[:],
                                                K.xr(tc, fs * 512, 512), ALU.add)
                        else:
                            i = v.tensor_tensor(K.xr(tc, fs * 512, 512),
                                                K.ps[bank][:],
                                                K.xr(tc, fs * 512, 512), ALU.add)
                        bank_hist[bank] = K.inc(i, "v", 1)
                    return _
                K.block.vector(mk_v())
        return K.snap("v")

    def dump(self, tag, items):
        if self.dbg_tag != tag:
            return False
        K = self

        def _(y):
            for e in ("t", "v", "a", "g"):
                K.w(y, e, K.snap(e))
            for (ap, c0, nc_, isb) in items:
                if not isb:
                    K.inc(y.dma_start(out=K.dbg[0:ap.shape[0], c0:c0 + nc_],
                                      in_=ap), "y", 16)
        K.block.sync(_)

        def _(g):
            K.w(g, "y", K.snap("y"))
            for (ap, c0, nc_, isb) in items:
                if isb:
                    K.inc(g.dma_start(out=K.dbg[0:ap.shape[0], c0:c0 + nc_],
                                      in_=ap), "g", 16)
            K.w(g, "g", K.snap("g"))
        K.block.gpsimd(_)
        self.store_out()
        return True

    def store_out(self):
        K = self

        def _(y):
            for e in ("t", "v", "a", "g"):
                K.w(y, e, K.snap(e))
            for tc in range(NTC):
                K.inc(y.dma_start(out=K.out[tc * 128:(tc + 1) * 128, :],
                                  in_=K.xr(tc)), "y", 16)
            K.w(y, "y", K.snap("y"))
        K.block.sync(_)


def build_program(upto=4, dbg_tag=None):
    K = KB(upto, dbg_tag)
    K.declare()
    K.stx.__enter__()
    try:
        K.alloc()
        emit_all(K)
    finally:
        K.stx.__exit__(None, None, None)
    return K.nc


def emit_all(K):
    upto = K.upto
    K.p0_setup()
    K.ln_stats()
    ev_xn0 = K.apply_norm(K.rcol)
    if K.dump("p1_xn", [(K.tm[:, 0:4096], 0, 4096, True),
                        (K.stats[:], 4096, 64, False)]):
        return
    ev_fm0 = K.transposes(lambda tc: K.tmv(tc, 0, D), ev_xn0)
    if K.dump("p1_fm", [(K.fm[:, 0:4096], 0, 4096, True)]):
        return

    ev_k0 = K.proj_tm(K.wd["l0wk"], 512, lambda tc, fs: K.tmv(tc, 0, 512), ev_fm0)
    ev_v0 = K.proj_tm(K.wd["l0wv"], 512, lambda tc, fs: K.tmv(tc, 512, 512),
                      ev_fm0)
    ev_kvw = 0
    ev_chain = ev_v0
    for tc in range(NTC):
        ev_rope = K.rope(0, 4, K.tmv(tc, 0, 512), ev_chain, tc)

        def mk_y(tc=tc, ev_rope=ev_rope):
            def _(y):
                K.w(y, "v", ev_rope)
                base = "kv0A" if tc < 4 else "kv0B"
                delta = (tc % 4) * 131072
                ap = K.dyn_ap(y, K.kv0[0:128, 0:512], base, delta, "kv0")
                K.inc(y.dma_start(out=ap, in_=K.ro[:, 0:512].bitcast(U8)), "y", 16)
                ap = K.dyn_ap(y, K.kv0[0:128, 0:512], base, delta + 512, "kv0")
                K.inc(y.dma_start(out=ap, in_=K.tmv(tc, 512, 512).bitcast(U8)), "y", 16)
            return _
        K.block.sync(mk_y())
        ev_kvw = K.snap("y")
        ev_chain = ("y", ev_kvw)
    K.flag_write(0, ev_kvw)

    ev_q0 = K.proj_tm(K.wd["l0wq"], D, lambda tc, fs: K.tmv(tc, fs * 512, 512),
                      ev_fm0, extra_first_wait=("y", ev_kvw))
    ev_qfm0 = _q_rope_transpose(K, 0, ev_q0)
    if K.dump("p3_qfm", [(K.fm[:, 0:4096], 0, 4096, True)]):
        return
    ev_poll0 = K.flag_poll(0)

    if upto < 1:
        K.store_out()
        return

    def _(y):
        K.w(y, "y", ev_poll0)
        K.wait_now(y, "g")
        K.wait_now(y, "t")
        for g in range(4):
            K.inc(y.dma_start_transpose(K.k0b(g), K.kv0[:, g * 128:(g + 1) * 128]),
                  "y", 16)
        for g in range(4):
            for kc in range(16):
                K.inc(y.dma_start(
                    out=K.v0b(g)[:, kc * 128:(kc + 1) * 128],
                    in_=K.kv0[kc * 128:(kc + 1) * 128,
                              512 + g * 128:512 + (g + 1) * 128]), "y", 16)
    K.block.sync(_)
    ev_kv0l = K.snap("y")
    if K.dump("p4_kv", [(K.k0b(0), 0, 2048, True), (K.v0b(0), 2048, 2048, True),
                        (K.k0b(1), 4096, 2048, True)]):
        return

    ev_att0 = K.attention(
        ev_kv0l, ev_qfm0,
        kfm_fn=lambda h, kc: K.k0b(h // 4)[:, kc * 128:(kc + 1) * 128],
        vtm_fn=lambda h, kc: K.v0b(h // 4)[:, kc * 128:(kc + 1) * 128],
        spans=[(0, list(range(16)), None), (1, list(range(16)), None)])
    K.proj_out(K.wd["l0wo"], ev_att0)

    if upto < 2:
        K.store_out()
        return
    emit_l0_ffn(K)
    if upto < 3:
        K.store_out()
        return
    emit_l1_attn(K)
    if upto < 4:
        K.store_out()
        return
    emit_l1_ffn(K)
    K.store_out()


def _q_rope_transpose(K, kind, ev_q):
    a_done = []
    nq = 0
    ev_chain = ev_q
    for tc in range(NTC):
        ev_rope = K.rope(kind, 16, K.tmv(tc, 0, D), ev_chain, tc)
        for grp in range(4):
            def mk_t(grp=grp, ev_rope=ev_rope, nq=nq):
                def _(t):
                    K.w(t, "v", ev_rope)
                    bank = 4 + nq % 2
                    if nq >= 2:
                        K.w(t, "a", a_done[nq - 2])
                    i = None
                    for j in range(4):
                        dc = grp * 4 + j
                        i = t.transpose(K.psb(bank, j * 128, 128),
                                        K.ro[:, dc * 128:(dc + 1) * 128],
                                        K.ident[:])
                    K.inc(i, "t", 1)
                return _
            K.block.tensor(mk_t())
            tsnap = K.snap("t")

            def mk_a(tc=tc, grp=grp, tsnap=tsnap, nq=nq):
                def _(a):
                    bank = 4 + nq % 2
                    K.w(a, "t", tsnap)
                    dst = sap(K.fm[:], (grp * 4) * 1024 + tc * 128,
                              [[NDC * 1024, 128], [1024, 4], [1, 128]])
                    a_done.append(K.inc(a.copy(dst, K.psb(bank, 0, 512)), "a", 1))
                return _
            K.block.scalar(mk_a())
            nq += 1
        ev_chain = ("t", K.snap("t"))
    return K.snap("a")


def emit_l0_ffn(K):
    anow, vnow = K.snap("a"), K.snap("v")

    def _(y):
        K.w(y, "a", anow)
        K.w(y, "v", vnow)
        K.inc(y.dma_start(out=K.ro32[0:1, 0:D], in_=K.l0fb), "y", 16)
    K.block.sync(_)
    ev_l0fb = K.snap("y")

    def _(v):
        K.w(v, "y", ev_l0fb)
        rf = K.ro32[0:1, 0:D]
        K.vchain(v, [
            lambda: v.tensor_reduce(K.srow[0:1, 0:1], rf, AX.X, ALU.max,
                                    apply_absolute_value=True),
            lambda: v.tensor_scalar(K.srow[0:1, 1:2], K.srow[0:1, 0:1],
                                    1.0 / 127.0, None, ALU.mult),
            lambda: v.tensor_scalar(K.srow[0:1, 2:3], K.srow[0:1, 0:1], 0.0,
                                    None, ALU.is_equal),
            lambda: v.tensor_scalar(K.srow[0:1, 4:5], K.srow[0:1, 2:3], -1.0,
                                    1.0, ALU.mult, ALU.add),
            lambda: v.tensor_tensor(K.srow[0:1, 1:2], K.srow[0:1, 1:2],
                                    K.srow[0:1, 4:5], ALU.mult),
            lambda: v.tensor_tensor(K.srow[0:1, 1:2], K.srow[0:1, 1:2],
                                    K.srow[0:1, 2:3], ALU.add),
            lambda: v.reciprocal(K.srow[0:1, 3:4], K.srow[0:1, 1:2]),
            lambda: v.tensor_scalar(rf, rf, K.srow[0:1, 3:4], None, ALU.mult),
            lambda: v.tensor_scalar(rf, rf, 127.0, -127.0, ALU.min, ALU.max),
            lambda: v.tensor_scalar(rf, rf, MAGIC, MAGIC, ALU.add, ALU.subtract),
            lambda: v.tensor_scalar(rf, rf, K.srow[0:1, 1:2], None, ALU.mult)])
    K.block.vector(_)
    ev_bq = K.snap("v")

    def _(y):
        K.w(y, "v", ev_bq)
        K.inc(y.dma_start(out=K.scr, in_=K.ro32[0:1, 0:D]), "y", 16)
    K.block.sync(_)
    yrow = K.snap("y")

    def _(g):
        K.w(g, "y", yrow)
        K.wait_now(g, "t")
        K.inc(g.dma_start(out=K.bqb[:], in_=sap(K.scr, 0, [[0, 128], [1, D]])),
              "g", 16)
    K.block.gpsimd(_)
    ev_bc2 = ("g", K.snap("g"))

    tt_hist = []
    for i in range(8):
        def mk_y(i=i):
            def _(y):
                if i == 0:
                    K.wait_now(y, "v")
                if i >= 2:
                    K.w(y, "v", tt_hist[i - 2])
                buf = K.ro32 if i % 2 == 0 else K.r32b
                ap = K.dyn_ap(y, K.wd["l0fk"][0:128, :], "l0fk", i * 262144, "l0fk")
                K.inc(y.dma_start(out=buf[:].bitcast(U8), in_=ap), "y", 16)
            return _
        K.block.sync(mk_y())
        ysnap = K.snap("y")

        def mk_a(i=i, ysnap=ysnap):
            def _(a):
                K.w(a, "y", ysnap)
                buf = K.ro32 if i % 2 == 0 else K.r32b
                K.inc(a.activation(buf[:], buf[:], AF.Abs), "a", 1)
            return _
        K.block.scalar(mk_a())
        asnap = K.snap("a")

        def mk_v(i=i, asnap=asnap):
            def _(v):
                K.w(v, "a", asnap)
                buf = K.ro32 if i % 2 == 0 else K.r32b
                if i == 0:
                    i_ = v.tensor_copy(K.sfb[:], buf[:])
                else:
                    i_ = v.tensor_tensor(K.sfb[:], K.sfb[:], buf[:], ALU.max)
                tt_hist.append(K.inc(i_, "v", 1))
            return _
        K.block.vector(mk_v())
    ev_macc = K.snap("v")

    def _(y):
        K.w(y, "v", ev_macc)
        K.inc(y.dma_start(out=K.idf32, in_=K.identf), "y", 16)
    K.block.sync(_)
    ev_idf = K.snap("y")

    red_hist = []
    a_prev = K.snap("a")
    for c in range(NDC):
        def mk_t(c=c):
            def _(t):
                if c == 0:
                    K.w(t, "v", ev_macc)
                    K.w(t, "y", ev_idf)
                    K.w(t, "a", a_prev)
                if c >= 2:
                    K.w(t, "v", red_hist[c - 2])
                i = t.transpose(K.ps[4 + c % 2][:, 0:128],
                                K.sfb[:, c * 128:(c + 1) * 128], K.idf32)
                K.inc(i, "t", 1)
            return _
        K.block.tensor(mk_t())
        tsnap = K.snap("t")

        def mk_v(c=c, tsnap=tsnap):
            def _(v):
                K.w(v, "t", tsnap)
                i = v.tensor_reduce(K.sca[:, 16 + c:17 + c],
                                    K.ps[4 + c % 2][:, 0:128], AX.X, ALU.max)
                red_hist.append(K.inc(i, "v", 1))
            return _
        K.block.vector(mk_v())
    ev_sfh = K.snap("v")

    def _(y):
        K.w(y, "v", ev_sfh)
        dst = K.dyn_ap(y, K.sfk_sh[0:1, :], "p8192", 0, "sfk")
        dst = dataclasses.replace(
            dst, ap=dst.ap.__class__([[4, 128], [512, 16], [1, 4]]))
        K.inc(y.dma_start(out=dst, in_=K.sca[:, 16:32].bitcast(U8),
                          allow_slow_non_contiguous=True), "y", 16)
    K.block.sync(_)
    K.flag_write(1, K.snap("y"))
    ev_pol = K.flag_poll(1)

    def _(y):
        K.w(y, "y", ev_pol)
        srcp = K.dyn_ap(y, K.sfk_sh[0:1, :], "n8192", 0, "sfk")
        srcp = dataclasses.replace(
            srcp, ap=srcp.ap.__class__([[4, 128], [512, 16], [1, 4]]))
        K.inc(y.dma_start(out=K.sca[:, 32:48].bitcast(U8), in_=srcp,
                          allow_slow_non_contiguous=True), "y", 16)
    K.block.sync(_)
    ev_sfl = K.snap("y")

    def _(v):
        K.w(v, "y", ev_sfl)
        K.vchain(v, [
            lambda: v.tensor_tensor(K.sca[:, 16:32], K.sca[:, 16:32],
                                    K.sca[:, 32:48], ALU.max),
            lambda: v.tensor_scalar(K.sca[:, 32:48], K.sca[:, 16:32],
                                    1.0 / 127.0, None, ALU.mult),
            lambda: v.tensor_scalar(K.sca[:, 48:64], K.sca[:, 16:32], 0.0,
                                    None, ALU.is_equal),
            lambda: v.tensor_scalar(K.sca[:, 0:16], K.sca[:, 48:64], -1.0, 1.0,
                                    ALU.mult, ALU.add),
            lambda: v.tensor_tensor(K.sca[:, 32:48], K.sca[:, 32:48],
                                    K.sca[:, 0:16], ALU.mult),
            lambda: v.tensor_tensor(K.sca[:, 32:48], K.sca[:, 32:48],
                                    K.sca[:, 48:64], ALU.add),
            lambda: v.reciprocal(K.sca[:, 48:64], K.sca[:, 32:48])])
    K.block.vector(_)
    ev_sfc = K.snap("v")

    def _(y):
        K.w(y, "v", ev_sfc)
        dst = sap(K.scr, 0, [[1, 128], [128, 16]])
        K.inc(y.dma_start(out=dst, in_=K.sca[:, 48:64],
                          allow_slow_non_contiguous=True), "y", 16)
        K.w(y, "y", K.snap("y"))
        K.inc(y.dma_start(out=K.sfb[:], in_=sap(K.scr, 0, [[0, 128], [1, D]])),
              "y", 16)
        dst = sap(K.scr, 0, [[1, 128], [128, 16]])
        K.inc(y.dma_start(out=dst, in_=K.sca[:, 32:48],
                          allow_slow_non_contiguous=True), "y", 16)
        K.w(y, "y", K.snap("y"))
        K.inc(y.dma_start(out=K.sfb2[:], in_=sap(K.scr, 0, [[0, 128], [1, D]])),
              "y", 16)
    K.block.sync(_)
    ev_bc1 = K.snap("y")

    if K.dump("l0ffn_scales", [(K.sfb[:], 0, 2048, False),
                               (K.sfb2[:], 2048, 2048, False),
                               (K.bqb[:], 4096, 2048, True),
                               (K.sca[:], 6144, 64, False),
                               (K.srow[:], 6208, 64, False)]):
        return
    K.ln_stats()

    def _(v):
        for tc in range(NTC):
            K.vchain(v, [
                lambda tc=tc: v.tensor_scalar(
                    K.ro32[:], K.xr(tc), K.mcol(tc), None, ALU.subtract,
                    ALU.max, accum_out=K.sca[:, 48:49]),
                lambda tc=tc: v.tensor_scalar(
                    K.ro32[:], K.xr(tc), K.mcol(tc), None, ALU.subtract,
                    ALU.min, accum_out=K.sca[:, 49:50]),
                lambda: v.tensor_scalar(K.sca[:, 49:50], K.sca[:, 49:50], -1.0,
                                        None, ALU.mult),
                lambda: v.tensor_tensor(K.sca[:, 48:49], K.sca[:, 48:49],
                                        K.sca[:, 49:50], ALU.max),
                lambda tc=tc: v.tensor_tensor(K.sca[:, 48:49], K.sca[:, 48:49],
                                              K.rcol(tc), ALU.mult),
                lambda: v.tensor_scalar(K.sca[:, 50:51], K.sca[:, 48:49],
                                        1.0 / 127.0, None, ALU.mult),
                lambda: v.tensor_scalar(K.sca[:, 51:52], K.sca[:, 48:49], 0.0,
                                        None, ALU.is_equal),
                lambda: v.tensor_scalar(K.sca[:, 53:54], K.sca[:, 51:52], -1.0,
                                        1.0, ALU.mult, ALU.add),
                lambda: v.tensor_tensor(K.sca[:, 50:51], K.sca[:, 50:51],
                                        K.sca[:, 53:54], ALU.mult),
                lambda tc=tc: v.tensor_tensor(K.sca[:, 8 + tc:9 + tc],
                                              K.sca[:, 50:51], K.sca[:, 51:52],
                                              ALU.add),
                lambda tc=tc: v.reciprocal(K.sca[:, 52:53],
                                           K.sca[:, 8 + tc:9 + tc]),
                lambda tc=tc: v.tensor_tensor(K.sca[:, 0 + tc:1 + tc],
                                              K.rcol(tc), K.sca[:, 52:53],
                                              ALU.mult),
                lambda tc=tc: v.tensor_scalar(K.ro32[:], K.xr(tc), K.mcol(tc),
                                              K.sca[:, 0 + tc:1 + tc],
                                              ALU.subtract, ALU.mult),
                lambda: v.tensor_scalar(K.ro32[:], K.ro32[:], 127.0, -127.0,
                                        ALU.min, ALU.max),
                lambda tc=tc: v.tensor_scalar(K.tmv(tc, 0, D), K.ro32[:],
                                              MAGIC, MAGIC, ALU.add,
                                              ALU.subtract)])
            if tc == 0:
                K.vchain(v, [lambda: v.tensor_copy(K.sca[:, 56:62],
                                                   K.sca[:, 48:54])])
    K.block.vector(_)
    ev_xint = K.snap("v")
    if K.dump("l0ffn_xint", [(K.tmv(0, 0, D), 0, 2048, True),
                             (K.sca[:], 2048, 64, False)]):
        return
    ev_xfm = K.transposes(lambda tc: K.tmv(tc, 0, D), ev_xint)

    bank_hist = {}
    ev_mm_span = {}
    tprev = K.snap("t")
    for fs in range(4):
        q_hist = []
        for dc in range(NDC):
            def mk_g(fs=fs, dc=dc):
                def _(g):
                    if dc == 0:
                        K.w(g, "t", ev_mm_span.get(fs - 1, tprev))
                    if dc >= 2:
                        K.w(g, "v", q_hist[dc - 2])
                    K.inc(g.dma_start(
                        out=K.wf32[:, (dc % 2) * 512:(dc % 2 + 1) * 512],
                        in_=K.wd["l0fk"][dc * 128:(dc + 1) * 128,
                                        fs * 512:(fs + 1) * 512]), "g", 16)
                return _
            K.block.gpsimd(mk_g())
            gsnap = K.snap("g")

            def mk_v(fs=fs, dc=dc, gsnap=gsnap):
                def _(v):
                    K.w(v, "g", gsnap)
                    if fs == 0 and dc == 0:
                        K.w(v, "y", ev_bc1)
                        K.w(v, *ev_bc2)
                    wt = K.wf32[:, (dc % 2) * 512:(dc % 2 + 1) * 512]
                    v.tensor_tensor(K.qscr[:], wt,
                                    K.sfb[:, fs * 512:(fs + 1) * 512], ALU.mult)
                    v.tensor_scalar(K.qscr[:], K.qscr[:], 127.0, -127.0,
                                    ALU.min, ALU.max)
                    i = v.tensor_scalar(K.wbuf[:, dc * 512:(dc + 1) * 512],
                                        K.qscr[:], MAGIC, MAGIC,
                                        ALU.add, ALU.subtract)
                    q_hist.append(K.inc(i, "v", 1))
                return _
            K.block.vector(mk_v())
        ev_q = K.snap("v")
        for tc in range(NTC):
            bank = tc % 4

            def mk_t(fs=fs, tc=tc, bank=bank, ev_q=ev_q):
                def _(t):
                    if tc == 0:
                        K.w(t, "v", ev_q)
                        if fs == 0:
                            K.w(t, "a", ev_xfm)
                    if bank in bank_hist:
                        K.w(t, "v", bank_hist[bank])
                    i = None
                    for dc in range(NDC):
                        i = t.matmul(K.ps[bank][:], K.fmv(dc, tc * 128, 128),
                                     K.wbuf[:, dc * 512:(dc + 1) * 512],
                                     start=(dc == 0), stop=(dc == NDC - 1))
                    K.inc(i, "t", 1)
                return _
            K.block.tensor(mk_t())
            ev_mm_span[fs] = K.snap("t")
            tsnap = K.snap("t")

            def mk_v2(fs=fs, tc=tc, bank=bank, tsnap=tsnap):
                def _(v):
                    K.w(v, "t", tsnap)
                    v.tensor_tensor(K.qscr[:], K.ps[bank][:],
                                    K.sfb2[:, fs * 512:(fs + 1) * 512], ALU.mult)
                    v.scalar_tensor_tensor(K.qscr[:], K.qscr[:],
                                           K.sca[:, 8 + tc:9 + tc],
                                           K.bqb[:, fs * 512:(fs + 1) * 512],
                                           ALU.mult, ALU.add)
                    i = v.tensor_scalar(K.tmv(tc, fs * 512, 512), K.qscr[:],
                                        0.0, None, ALU.max, ALU.max,
                                        accum_out=K.sca[:, 16 + tc * 4 + fs:
                                                        17 + tc * 4 + fs])
                    bank_hist[bank] = K.inc(i, "v", 1)
                return _
            K.block.vector(mk_v2())
    ev_y = K.snap("v")
    if K.dump("l0ffn_y", [(K.tmv(0, 0, D), 0, 2048, True),
                          (K.wbuf[:, 0:2048], 2048, 2048, True),
                          (K.sca[:], 4096, 64, False)]):
        return

    def _(v):
        for tc in range(NTC):
            K.vchain(v, [
                lambda tc=tc: v.tensor_reduce(
                    K.sca[:, 48:49], K.sca[:, 16 + tc * 4:20 + tc * 4], AX.X,
                    ALU.max),
                lambda: v.tensor_scalar(K.sca[:, 50:51], K.sca[:, 48:49],
                                        1.0 / 127.0, None, ALU.mult),
                lambda: v.tensor_scalar(K.sca[:, 51:52], K.sca[:, 48:49], 0.0,
                                        None, ALU.is_equal),
                lambda: v.tensor_scalar(K.sca[:, 53:54], K.sca[:, 51:52], -1.0,
                                        1.0, ALU.mult, ALU.add),
                lambda: v.tensor_tensor(K.sca[:, 50:51], K.sca[:, 50:51],
                                        K.sca[:, 53:54], ALU.mult),
                lambda: v.tensor_tensor(K.sca[:, 48:49], K.sca[:, 50:51],
                                        K.sca[:, 51:52], ALU.add),
                lambda: v.reciprocal(K.sca[:, 49:50], K.sca[:, 48:49])])
            for fs in range(4):
                K.vchain(v, [
                    lambda tc=tc, fs=fs: v.tensor_scalar(
                        K.qscr[:], K.tmv(tc, fs * 512, 512), K.sca[:, 49:50],
                        None, ALU.mult),
                    lambda: v.tensor_scalar(K.qscr[:], K.qscr[:], MAGIC, MAGIC,
                                            ALU.add, ALU.subtract),
                    lambda tc=tc, fs=fs: v.scalar_tensor_tensor(
                        K.xr(tc, fs * 512, 512), K.qscr[:], K.sca[:, 48:49],
                        K.xr(tc, fs * 512, 512), ALU.mult, ALU.add)])
    K.block.vector(_)


def emit_l1_attn(K):
    vnow = K.snap("v")

    def _(y):
        K.w(y, "v", vnow)
        K.inc(y.dma_start(out=K.tabs[:], in_=K.t_l1), "y", 16)
        for i in range(16):
            K.inc(y.dma_start(out=K.maskv(i),
                              in_=K.mkin[:, i * 512:(i + 1) * 512]), "y", 16)
    K.block.sync(_)
    ev_tabs = K.snap("y")

    K.ln_stats(extra_rms=True)
    ev_xr = K.apply_norm(K.rccol)

    def _(v):
        K.w(v, "v", ev_xr)
        for tc in range(NTC):
            i = v.tensor_scalar(K.xr(tc), K.xr(tc), K.sca[:, 32 + tc:33 + tc],
                                K.sca[:, 40 + tc:41 + tc], ALU.mult, ALU.add)
            K.inc(i, "v", 1)
    K.block.vector(_)
    ev_fm1 = K.transposes(lambda tc: K.tmv(tc, 0, D), ev_xr,
                          scol_fn=lambda dc: K.rmsc[:, dc:dc + 1])

    ev_k1 = K.proj_tm(K.wd["l1wk"], D, lambda tc, fs: K.tmv(tc, fs * 512, 512),
                      ev_fm1, bias_dram=K.brow["bk"])
    ev_kw = 0
    ev_chain = (("y", ev_tabs), ev_k1)

    def _(v):
        K.w(v, "y", ev_tabs)
        K.inc(v.nop(), "v", 1)
    K.block.vector(_)
    ev_chain = max(ev_k1, K.snap("v"))
    for tc in range(NTC):
        ev_rope = K.rope(1, 16, K.tmv(tc, 0, D), ev_chain, tc)

        def mk_y(tc=tc, ev_rope=ev_rope):
            def _(y):
                K.w(y, "v", ev_rope)
                base = "kv1A" if tc < 4 else "kv1B"
                delta = (tc % 4) * 524288
                ap = K.dyn_ap(y, K.kv1[0:128, 0:2048], base, delta, "kv1")
                K.inc(y.dma_start(out=ap, in_=K.ro[:].bitcast(U8)), "y", 16)
            return _
        K.block.sync(mk_y())
        ev_kw = K.snap("y")
        ev_chain = ("y", ev_kw)
    ev_v1 = K.proj_tm(K.wd["l1wv"], D, lambda tc, fs: K.tmv(tc, fs * 512, 512),
                      ev_fm1, bias_dram=K.brow["bv"], extra_first_wait=("y", ev_kw))
    ev_vw = 0
    for tc in range(NTC):
        def mk_y(tc=tc):
            def _(y):
                K.w(y, "v", ev_v1)
                base = "kv1A" if tc < 4 else "kv1B"
                delta = (tc % 4) * 524288 + 2048
                ap = K.dyn_ap(y, K.kv1[0:128, 0:2048], base, delta, "kv1")
                K.inc(y.dma_start(out=ap, in_=K.tmv(tc, 0, D).bitcast(U8)), "y", 16)
            return _
        K.block.sync(mk_y())
        ev_vw = K.snap("y")
    K.flag_write(2, ev_vw)
    ev_q1 = K.proj_tm(K.wd["l1wq"], D, lambda tc, fs: K.tmv(tc, fs * 512, 512),
                      ev_fm1, bias_dram=K.brow["bq"], extra_first_wait=("y", ev_vw))
    ev_qfm1 = _q_rope_transpose(K, 1, ev_q1)
    ev_poll = K.flag_poll(2)

    K.ev_headload = {}
    head_t_end = {}

    def head_load(h):
        if h >= 1:
            head_t_end[h - 1] = K.snap("t")

        def _(y):
            if h == 0:
                K.w(y, "y", ev_poll)
                K.wait_now(y, "g")
                K.wait_now(y, "t")
            if h >= 2:
                K.w(y, "t", head_t_end[h - 1])
            sl = h % 2
            K.inc(y.dma_start_transpose(K.k1r(sl),
                                        K.kv1[:, h * 128:(h + 1) * 128]), "y", 16)
            for kc in range(16):
                K.inc(y.dma_start(
                    out=K.v1r(sl)[:, kc * 128:(kc + 1) * 128],
                    in_=K.kv1[kc * 128:(kc + 1) * 128,
                              2048 + h * 128:2048 + (h + 1) * 128]), "y", 16)
        K.block.sync(_)
        K.ev_headload[h] = K.snap("y")

    spans = [(0, list(range(8)), lambda kc: kc),
             (1, list(range(16)), lambda kc: kc if kc >= 8 else None)]
    ev_att1 = K.attention(ev_poll, ev_qfm1,
                          kfm_fn=lambda h, kc: K.k1r(h % 2)[:, kc * 128:
                                                           (kc + 1) * 128],
                          vtm_fn=lambda h, kc: K.v1r(h % 2)[:, kc * 128:
                                                           (kc + 1) * 128],
                          spans=spans, head_load=head_load)
    K.proj_out(K.wd["l1wo"], ev_att1, bias_dram=K.brow["bo"])


def _ffn_plan():
    plan = []
    for nm in ("w1", "w2"):
        for i in range(22):
            plan.append((nm, i, 0, 2048))
    for i in range(8):
        for (c0, cn) in ((0, 2048), (2048, 2048), (4096, 1536)):
            plan.append(("w3", i, c0, cn))
    return plan


def _half_base(nm, par):
    return par * (2816 if nm != "w3" else 1024)


def emit_l1_ffn(K):
    plan = _ffn_plan()
    mi_of = {"w1": 0, "w2": 1, "w3": 2}

    def _(v):
        K.wait_now(v, "a")
        K.inc(v.memset(K.gcol[:, 0:3], 0.0), "v", 1)
    K.block.vector(_)
    acc_hist = []
    for n, (nm, i, c0, cn) in enumerate(plan):
        def mk_y(n=n, nm=nm, i=i, c0=c0, cn=cn):
            def _(y):
                if n == 0:
                    K.wait_now(y, "v")
                if n >= 2:
                    K.w(y, "v", acc_hist[n - 2])
                buf = K.ro32 if n % 2 == 0 else K.r32b
                rowel = D if nm != "w3" else HFF
                bcl = "w12" if nm != "w3" else ("w3a" if cn == 2048 else "w3b")
                ap = K.dyn_ap(y, K.wd[nm][0:128, c0:c0 + cn], "pbig4",
                              i * 128 * rowel + c0, bcl)
                K.inc(y.dma_start(out=buf[:, 0:cn].bitcast(U8), in_=ap), "y", 16)
            return _
        K.block.sync(mk_y())
        ysnap = K.snap("y")

        def mk_a(n=n, cn=cn, ysnap=ysnap):
            def _(a):
                K.w(a, "y", ysnap)
                buf = K.ro32 if n % 2 == 0 else K.r32b
                K.inc(a.activation(buf[:, 0:cn], buf[:, 0:cn], AF.Abs,
                                   accum_out=K.gcol[:, 4:5]), "a", 1)
            return _
        K.block.scalar(mk_a())
        asnap = K.snap("a")

        def mk_v(n=n, nm=nm, asnap=asnap):
            def _(v):
                K.w(v, "a", asnap)
                mi = mi_of[nm]
                i_ = v.tensor_tensor(K.gcol[:, mi:mi + 1], K.gcol[:, mi:mi + 1],
                                     K.gcol[:, 4:5], ALU.add)
                acc_hist.append(K.inc(i_, "v", 1))
            return _
        K.block.vector(mk_v())
    ev_gacc = K.snap("v")

    def _(y):
        K.w(y, "v", ev_gacc)
        K.inc(y.dma_start(out=K.idf32, in_=K.identf), "y", 16)
    K.block.sync(_)
    ev_idf2 = K.snap("y")

    a_prev = K.snap("a")
    for mi in range(3):
        def mk_t(mi=mi):
            def _(t):
                if mi == 0:
                    K.w(t, "v", ev_gacc)
                    K.w(t, "y", ev_idf2)
                    K.w(t, "a", a_prev)
                i = t.transpose(K.ps[4][0:1, mi * 128:(mi + 1) * 128],
                                K.gcol[:, mi:mi + 1], K.idf32)
                K.inc(i, "t", 1)
            return _
        K.block.tensor(mk_t())
    tsnap = K.snap("t")

    def _(v):
        K.w(v, "t", tsnap)
        for mi in range(3):
            i = v.tensor_reduce(K.srow[0:1, 8 + mi:9 + mi],
                                K.ps[4][0:1, mi * 128:(mi + 1) * 128],
                                AX.X, ALU.add)
            K.inc(i, "v", 1)
        K.w(v, "v", K.snap("v"))
    K.block.vector(_)
    ev_gred = K.snap("v")

    def _(y):
        K.w(y, "v", ev_gred)
        ap = K.dyn_ap(y, K.gam_sh[0:1, 0:3], "p64", 0, "gam")
        K.inc(y.dma_start(out=ap, in_=K.srow[0:1, 8:11].bitcast(U8)), "y", 16)
    K.block.sync(_)
    K.flag_write(3, K.snap("y"))
    ev_pol3 = K.flag_poll(3)

    def _(y):
        K.w(y, "y", ev_pol3)
        ap = K.dyn_ap(y, K.gam_sh[0:1, 0:3], "n64", 0, "gam")
        K.inc(y.dma_start(out=K.srow[0:1, 16:19].bitcast(U8), in_=ap), "y", 16)
    K.block.sync(_)
    ev_gl = K.snap("y")

    def _(v):
        K.w(v, "y", ev_gl)
        K.vchain(v, [
            lambda: v.tensor_tensor(K.srow[0:1, 8:11], K.srow[0:1, 8:11],
                                    K.srow[0:1, 16:19], ALU.add),
            lambda: v.tensor_scalar(K.srow[0:1, 8:11], K.srow[0:1, 8:11],
                                    1.0 / NMEL, 1e-8, ALU.mult, ALU.add),
            lambda: v.reciprocal(K.srow[0:1, 24:27], K.srow[0:1, 8:11])])
    K.block.vector(_)
    ev_gr = K.snap("v")

    def _(y):
        K.w(y, "v", ev_gr)
        K.inc(y.dma_start(out=K.scr[0:1, 0:3], in_=K.srow[0:1, 24:27]), "y", 16)
        K.w(y, "y", K.snap("y"))
        for mi in range(3):
            K.inc(y.dma_start(out=K.gcol[:, 5 + mi:6 + mi],
                              in_=sap(K.scr, mi, [[0, 128], [1, 1]])), "y", 16)
    K.block.sync(_)
    ev_gb = K.snap("y")

    q_hist = []
    st_hist = []
    for n, (nm, i, c0, cn) in enumerate(plan):
        def mk_y(n=n, nm=nm, i=i, c0=c0, cn=cn):
            def _(y):
                if n == 0:
                    K.w(y, "y", ev_gb)
                if n >= 2:
                    K.w(y, "v", q_hist[n - 2])
                buf = K.ro32 if n % 2 == 0 else K.r32b
                rowel = D if nm != "w3" else HFF
                bcl = "w12" if nm != "w3" else ("w3a" if cn == 2048 else "w3b")
                ap = K.dyn_ap(y, K.wd[nm][0:128, c0:c0 + cn], "pbig4",
                              i * 128 * rowel + c0, bcl)
                K.inc(y.dma_start(out=buf[:, 0:cn].bitcast(U8), in_=ap), "y", 16)
            return _
        K.block.sync(mk_y())
        ysnap = K.snap("y")

        def mk_v(n=n, nm=nm, cn=cn, ysnap=ysnap):
            def _(v):
                K.w(v, "y", ysnap)
                if n >= 2:
                    K.w(v, "y", st_hist[n - 2])
                buf = K.ro32 if n % 2 == 0 else K.r32b
                mi = mi_of[nm]
                v.tensor_scalar(buf[:, 0:cn], buf[:, 0:cn],
                                K.gcol[:, 5 + mi:6 + mi], 1.0, ALU.mult, ALU.min)
                v.tensor_scalar(buf[:, 0:cn], buf[:, 0:cn], -1.0, MAGIC,
                                ALU.max, ALU.add)
                i_ = v.tensor_scalar(K.tm[:, (n % 2) * D:(n % 2) * D + cn],
                                     buf[:, 0:cn], MAGIC, None, ALU.subtract)
                q_hist.append(K.inc(i_, "v", 1))
            return _
        K.block.vector(mk_v())
        vsnap = K.snap("v")

        def mk_y2(n=n, nm=nm, i=i, c0=c0, cn=cn, vsnap=vsnap):
            def _(y):
                K.w(y, "v", vsnap)
                rowel = D if nm != "w3" else HFF
                bcl = "w12s" if nm != "w3" else ("w3as" if cn == 2048 else "w3bs")
                ap = K.dyn_ap(y, K.wq_sh[nm][0:128, c0:c0 + cn], "pbig2",
                              i * 128 * rowel + c0, bcl)
                K.inc(y.dma_start(
                    out=ap, in_=K.tm[:, (n % 2) * D:(n % 2) * D + cn].bitcast(U8)), "y", 16)
                st_hist.append(K.snap("y"))
            return _
        K.block.sync(mk_y2())
    K.flag_write(4, K.snap("y"))
    ev_pol4 = K.flag_poll(4)
    if K.dump("l1ffn_q", [(K.gcol[:, 0:8], 0, 8, False),
                          (K.srow[:], 8, 64, False),
                          (K.wq_sh["w1"][0:128, 0:2048], 1024, 2048, True),
                          (K.wq_sh["w1"][2816:2944, 0:2048], 3072, 2048, True),
                          (K.wq_sh["w3"][0:128, 0:2048], 5120, 2048, True)]):
        return

    K.ln_stats()
    ev_xn2 = K.apply_norm(K.rcol)
    ev_fm2 = K.transposes(lambda tc: K.tmv(tc, 0, D), ev_xn2)

    groups = [(0, 4), (4, 4), (8, 3)]
    bank_hist = {}
    for H in range(4):
        grp_t_end = {}
        hid_v = 0
        for gi, (g0, gn) in enumerate(groups):
            def mk_y(H=H, gi=gi, g0=g0, gn=gn):
                def _(y):
                    if H == 0 and gi == 0:
                        K.w(y, "y", ev_pol4)
                        K.w(y, "a", K.snap("a"))
                        K.w(y, "v", K.snap("v"))
                    if (H, gi) != (0, 0):
                        K.w(y, "t", grp_t_end.get(gi - 1, K.snap("t")))
                    rows0 = H * 1408 + g0 * 128
                    for dc in range(NDC):
                        K.inc(y.dma_start_transpose(
                            K.wbuf[:, dc * 512:dc * 512 + gn * 128],
                            K.wq_sh["w1"][rows0:rows0 + gn * 128,
                                          dc * 128:(dc + 1) * 128]), "y", 16)
                    for dc in range(NDC):
                        K.inc(y.dma_start_transpose(
                            K.w2buf[:, dc * 512:dc * 512 + gn * 128],
                            K.wq_sh["w2"][rows0:rows0 + gn * 128,
                                          dc * 128:(dc + 1) * 128]), "y", 16)
                return _
            K.block.sync(mk_y())
            ev_wt = K.snap("y")
            for hc_l in range(gn):
                for sp in range(2):
                    it = hc_l * 2 + sp
                    b1, b2 = (0, 1) if it % 2 == 0 else (2, 3)

                    def mk_t(H=H, gi=gi, hc_l=hc_l, sp=sp, b1=b1, b2=b2,
                             ev_wt=ev_wt):
                        def _(t):
                            if hc_l == 0 and sp == 0:
                                K.w(t, "y", ev_wt)
                                if H == 0 and gi == 0:
                                    K.w(t, "a", ev_fm2)
                            if b1 in bank_hist:
                                K.w(t, "v", bank_hist[b1])
                            for dc in range(NDC):
                                t.matmul(K.ps[b1][:],
                                         K.wbuf[:, dc * 512 + hc_l * 128:
                                                dc * 512 + (hc_l + 1) * 128],
                                         K.fmv(dc, sp * 512, 512),
                                         start=(dc == 0), stop=(dc == NDC - 1))
                            i = None
                            for dc in range(NDC):
                                i = t.matmul(K.ps[b2][:],
                                             K.w2buf[:, dc * 512 + hc_l * 128:
                                                     dc * 512 + (hc_l + 1) * 128],
                                             K.fmv(dc, sp * 512, 512),
                                             start=(dc == 0), stop=(dc == NDC - 1))
                            K.inc(i, "t", 1)
                        return _
                    K.block.tensor(mk_t())
                    grp_t_end[gi] = K.snap("t")
                    tsnap = K.snap("t")

                    def mk_a(b1=b1, tsnap=tsnap):
                        def _(a):
                            K.w(a, "t", tsnap)
                            K.inc(a.activation(K.ro[:, 0:512], K.ps[b1][:],
                                               AF.Sigmoid), "a", 1)
                        return _
                    K.block.scalar(mk_a())
                    asnap = K.snap("a")

                    def mk_v(g0=g0, hc_l=hc_l, sp=sp, b1=b1, b2=b2, asnap=asnap):
                        def _(v):
                            K.w(v, "a", asnap)
                            hc = g0 + hc_l
                            i = v.tensor_tensor(
                                K.tm[:, hc * 1024 + sp * 512:
                                     hc * 1024 + (sp + 1) * 512],
                                K.ro[:, 0:512], K.ps[b2][:], ALU.mult)
                            bank_hist[b1] = K.inc(i, "v", 1)
                            bank_hist[b2] = bank_hist[b1]
                        return _
                    K.block.vector(mk_v())
        ev_hid = K.snap("v")
        if H == 0 and K.dump("l1ffn_h", [(K.tm[:, 0:1024], 0, 1024, True),
                                         (K.fm[:, 0:1024], 1024, 1024, True),
                                         (K.tm[:, 1024:2048], 2048, 1024, True),
                                         (K.xr(0), 4096, 2048, False),
                                         (K.xr(4), 6144, 2048, False)]):
            return

        ds_t_end = {}
        for dsp in range(4):
            def mk_y(H=H, dsp=dsp, ev_hid=ev_hid):
                def _(y):
                    if dsp == 0:
                        K.w(y, "v", ev_hid)
                    else:
                        K.w(y, "t", ds_t_end[dsp - 1])
                    for hc in range(NQH):
                        K.inc(y.dma_start_transpose(
                            K.wt3[:, hc * 512:(hc + 1) * 512],
                            K.wq_sh["w3"][dsp * 512:(dsp + 1) * 512,
                                          H * 1408 + hc * 128:
                                          H * 1408 + (hc + 1) * 128]), "y", 16)
                return _
            K.block.sync(mk_y())
            ev_w3 = K.snap("y")
            for tc in range(NTC):
                bank = tc % 4

                def mk_t(H=H, dsp=dsp, tc=tc, bank=bank, ev_w3=ev_w3):
                    def _(t):
                        if tc == 0:
                            K.w(t, "y", ev_w3)
                        if bank in bank_hist:
                            K.w(t, "v", bank_hist[bank])
                        i = None
                        for hc in range(NQH):
                            i = t.matmul(K.ps[bank][:],
                                         K.tm[:, hc * 1024 + tc * 128:
                                              hc * 1024 + (tc + 1) * 128],
                                         K.wt3[:, hc * 512:(hc + 1) * 512],
                                         start=(hc == 0), stop=(hc == NQH - 1))
                        K.inc(i, "t", 1)
                    return _
                K.block.tensor(mk_t())
                ds_t_end[dsp] = K.snap("t")
                tsnap = K.snap("t")

                def mk_v(dsp=dsp, tc=tc, bank=bank, tsnap=tsnap):
                    def _(v):
                        K.w(v, "t", tsnap)
                        i = v.tensor_tensor(K.xr(tc, dsp * 512, 512),
                                            K.ps[bank][:],
                                            K.xr(tc, dsp * 512, 512), ALU.add)
                        bank_hist[bank] = K.inc(i, "v", 1)
                    return _
                K.block.vector(mk_v())


def _host_constants(par):
    import ml_dtypes
    toks = np.concatenate([np.arange(par * 512, par * 512 + 512),
                           np.arange((3 - par) * 512, (3 - par) * 512 + 512)])
    j = np.arange(128)
    invf = 10000.0 ** (-(2.0 * (j % 64)) / 128.0)
    ang = toks[:, None].astype(np.float64) * invf[None, :]
    cos0 = np.cos(ang).astype(np.float32)
    sgn = np.where(j < 64, -1.0, 1.0).astype(np.float32)
    cos0s = cos0 * sgn[None, :]
    ca, sa = np.cos(ang), np.sin(ang)
    even = (j % 2 == 0)[None, :]
    A1 = np.where(even, ca, sa).astype(np.float32)
    B1 = np.where(even, -sa, ca).astype(np.float32)

    def cm(a, b):
        cc = []
        for t in (a, b):
            cc.append(np.concatenate([t[c * 128:(c + 1) * 128, :]
                                      for c in range(8)], axis=1))
        return np.ascontiguousarray(np.concatenate(cc, axis=1))

    qa0, qb0 = par * 512, (3 - par) * 512
    p = np.arange(128)[:, None]
    qf = np.arange(512)[None, :]
    mka = [(kc * 128 + p <= qa0 + qf).astype(np.float32) for kc in range(8)]
    mkb = [(kc * 128 + p <= qb0 + qf).astype(np.float32) for kc in range(8, 16)]
    mk = np.concatenate(mka + mkb, axis=1)
    return {
        "t_l0": cm(cos0, cos0s), "t_l1": cm(A1, B1),
        "mkin": np.ascontiguousarray(mk.astype(ml_dtypes.bfloat16)),
    }, toks


def make_inputs(inputs, tok_val):
    x = np.ascontiguousarray(np.asarray(inputs["L0_attn_x"], dtype=np.float32))
    tok = np.broadcast_to(
        (np.arange(8, dtype=np.int64)[:, None] + tok_val).astype(np.int32),
        (8, 16)).copy()
    rms = np.ascontiguousarray(
        np.asarray(inputs["L1_attn_rms_w"], dtype=np.float32).reshape(16, 128).T)
    f32 = lambda a: np.ascontiguousarray(np.asarray(a, dtype=np.float32))
    common = {
        "l0wq": f32(inputs["L0_attn_Wq"]), "l0wk": f32(inputs["L0_attn_Wk"]),
        "l0wv": f32(inputs["L0_attn_Wv"]), "l0wo": f32(inputs["L0_attn_Wo"]),
        "l0fk": f32(inputs["L0_ffn_kernel"]),
        "l0fb": f32(inputs["L0_ffn_bias"]).reshape(1, -1),
        "rmsw": rms,
        "l1wq": f32(inputs["L1_attn_wq"]), "l1wk": f32(inputs["L1_attn_wk"]),
        "l1wv": f32(inputs["L1_attn_wv"]), "l1wo": f32(inputs["L1_attn_wo"]),
        "bq": f32(inputs["L1_attn_bq"]).reshape(1, -1),
        "bk": f32(inputs["L1_attn_bk"]).reshape(1, -1),
        "bv": f32(inputs["L1_attn_bv"]).reshape(1, -1),
        "bo": f32(inputs["L1_attn_bo"]).reshape(1, -1),
        "w1": f32(inputs["L1_ffn_w1"]), "w2": f32(inputs["L1_ffn_w2"]),
        "w3": f32(inputs["L1_ffn_w3"]),
        "identf": np.eye(128, dtype=np.float32), "tokin": tok,
    }
    in_maps, metas = [], []
    for c in range(8):
        par = c % 2
        consts, toks = _host_constants(par)
        m = dict(common)
        m.update(consts)
        m["xin"] = np.ascontiguousarray(x[c // 2, toks, :])
        in_maps.append(m)
        metas.append((c // 2, toks))
    return in_maps, metas


def run(inputs, upto=4, dbg_tag=None):
    tok_val = int(np.random.default_rng().integers(1, 2 ** 30))
    in_maps, metas = make_inputs(inputs, tok_val)
    nc = build_program(upto, dbg_tag)
    res = run_bass_kernel_spmd(nc, in_maps, core_ids=list(range(8)))
    if dbg_tag is not None:
        return res
    out = np.zeros((4, 2048, 2048), dtype=np.float32)
    for c in range(8):
        b, toks = metas[c]
        out[b, toks, :] = res.results[c]["out"]
    return out


def kernel(**inputs):
    return run(inputs, upto=4)
